# version 10 (final)
# speedup vs baseline: 1.3907x; 1.0071x over previous
import math
from contextlib import ExitStack

import numpy as np
import ml_dtypes

import concourse.bass as bass
import concourse.mybir as mybir
from concourse.bass_utils import run_bass_kernel_spmd

F32 = mybir.dt.float32
F16 = mybir.dt.float16
BF16 = mybir.dt.bfloat16
AF = mybir.ActivationFunctionType
ALU = mybir.AluOpType
AX = mybir.AxisListType.X

SEQ = 2048
DM = 1024
HEADS = 16
TOPK = 204
TARGET = 199.5
NT = 16
B_PER_CORE = 4
MASKVAL = -30000.0
KEEPALL_T = -29000.0
NB = 12
VSTRIDE = 66
VBLK = HEADS * VSTRIDE
GRP = [(1, 8), (9, 12), (13, 15)]
ACT_TILES = (6, 7, 8, 11, 12, 14, 15)
GSL = {1: (slice(1, 6), slice(6, 9)),
       9: (slice(9, 11), slice(11, 13)),
       13: (slice(13, 14), slice(14, 16))}
SWAPMASK = [1, 0, 3, 2, 5, 4, 7, 6, 9, 8, 11, 10, 13, 12, 15, 14,
            17, 16, 19, 18, 21, 20, 23, 22, 25, 24, 27, 26, 29, 28, 31, 30]

TAPS = set()


def MM(e, *a, **kw):
    kw.setdefault("skip_group_check", True)
    return e.matmul(*a, **kw)


def norm_ppf(p):
    p = np.asarray(p, dtype=np.float64)
    a = [-3.969683028665376e+01, 2.209460984245205e+02, -2.759285104469687e+02,
         1.383577518672690e+02, -3.066479806614716e+01, 2.506628277459239e+00]
    b = [-5.447609879822406e+01, 1.615858368580409e+02, -1.556989798598866e+02,
         6.680131188771972e+01, -1.328068155288572e+01]
    c = [-7.784894002430293e-03, -3.223964580411365e-01, -2.400758277161838e+00,
         -2.549732539343734e+00, 4.374664141464968e+00, 2.938163982698783e+00]
    d = [7.784695709041462e-03, 3.224671290700398e-01, 2.445134137142996e+00,
         3.754408661907416e+00]
    plow, phigh = 0.02425, 1 - 0.02425
    out = np.empty_like(p)
    lo = p < plow
    hi = p > phigh
    mid = ~(lo | hi)
    if mid.any():
        q = p[mid] - 0.5
        r = q * q
        out[mid] = ((((((a[0] * r + a[1]) * r + a[2]) * r + a[3]) * r + a[4])
                     * r + a[5]) * q /
                    (((((b[0] * r + b[1]) * r + b[2]) * r + b[3]) * r + b[4])
                     * r + 1.0))
    if lo.any():
        q = np.sqrt(-2 * np.log(p[lo]))
        out[lo] = (((((c[0] * q + c[1]) * q + c[2]) * q + c[3]) * q + c[4])
                   * q + c[5]) / \
                  ((((d[0] * q + d[1]) * q + d[2]) * q + d[3]) * q + 1.0)
    if hi.any():
        q = np.sqrt(-2 * np.log(1 - p[hi]))
        out[hi] = -((((((c[0] * q + c[1]) * q + c[2]) * q + c[3]) * q + c[4])
                     * q + c[5]) /
                    ((((d[0] * q + d[1]) * q + d[2]) * q + d[3]) * q + 1.0))
    return out


def build_consts():
    C = {}
    n = (np.arange(SEQ) + 1).astype(np.float64).reshape(NT, 128).T
    frac = np.clip(TARGET / n, 1e-9, 1 - 1e-9)
    z0 = norm_ppf(1 - frac)
    an_inv = 1.0 / np.sqrt(2 * np.log(np.maximum(n, 2.0)))
    C["z_lo"] = (z0 - 0.5).astype(np.float32)
    C["z_hi"] = (z0 + 0.5).astype(np.float32)
    C["neg_an_inv"] = (-an_inv).astype(np.float32)
    C["inv_T0"] = np.broadcast_to(
        (1.0 / (128.0 * np.maximum(np.arange(NT), 1))).astype(np.float32),
        (128, NT)).copy()
    keep = (n.astype(np.int64) > TOPK).astype(np.float32)
    C["decmask"] = keep
    C["decoff"] = ((1.0 - keep) * KEEPALL_T).astype(np.float32)
    q_i = np.arange(128)[:, None]
    j_i = np.arange(128)[None, :]
    C["diagmask"] = np.where(j_i <= q_i, 0.0, MASKVAL).astype(np.float32)
    jj = np.arange(128)[:, None]
    qq = np.arange(512)[None, :]
    for oi, off in enumerate((0, 128, 256, 384)):
        C[f"causal01_{oi}"] = ((jj + off) <= qq).astype(ml_dtypes.bfloat16)
    invf = 1.0 / 10000.0 ** (np.arange(0, 64, 2, dtype=np.float64) / 64)
    t = np.arange(SEQ, dtype=np.float64)
    ang = t[None, :] * invf[:, None]
    cosf = np.cos(ang)
    sinf = np.sin(ang)
    C2 = np.empty((128, SEQ), dtype=np.float64)
    S2 = np.empty((128, SEQ), dtype=np.float64)
    for dd in range(128):
        i = (dd % 64) // 2
        C2[dd] = cosf[i]
        S2[dd] = -sinf[i] if (dd % 2 == 0) else sinf[i]
    C["rope_c2"] = C2.astype(np.float32)
    C["rope_s2"] = S2.astype(np.float32)
    C["trow"] = np.broadcast_to(np.arange(SEQ, dtype=np.float32),
                                (128, SEQ)).copy()
    C["ident"] = np.eye(128, dtype=np.float32)
    C["iota16"] = np.broadcast_to(np.arange(16, dtype=np.float32),
                                  (128, 16)).copy()
    thr = np.full(16, 1e30)
    for T in ACT_TILES:
        thr[T] = 128.0 * (T + 1) - 398.5
    C["cthr"] = np.broadcast_to(thr.astype(np.float32), (128, 16)).copy()
    return C


class Ctx:

    def __init__(self):
        self.qs = []
        self.prog = {}
        self.count = {}
        self.seen = {}
        self.last_w = {}
        self.readers = {}

    def add_queue(self, name):
        self.qs.append(name)
        self.prog[name] = []
        self.count[(name, "op")] = 0
        self.count[(name, "dma")] = 0
        self.seen[name] = {}

    def _wait(self, q, src, kind, val):
        if val <= 0:
            return
        if src == q and q == "ten":
            return
        key = (src, kind)
        if self.seen[q].get(key, 0) >= val:
            return
        self.prog[q].append(("wait", src, kind, val))
        self.seen[q][key] = val

    def _emit(self, q, kind, fn, reads, writes):
        for r in reads:
            lw = self.last_w.get(r)
            if lw is not None:
                self._wait(q, lw[0], lw[1], lw[2])
        for w in writes:
            lw = self.last_w.get(w)
            if lw is not None:
                self._wait(q, lw[0], lw[1], lw[2])
            for (rq, rk), rc in self.readers.get(w, {}).items():
                self._wait(q, rq, rk, rc)
        self.prog[q].append((kind, fn))
        self.count[(q, kind)] += 16 if kind == "dma" else 1
        cnt = self.count[(q, kind)]
        for r in reads:
            self.readers.setdefault(r, {})
            self.readers[r][(q, kind)] = cnt
        for w in writes:
            self.last_w[w] = (q, kind, cnt)
            self.readers[w] = {}

    def op(self, q, fn, reads=(), writes=()):
        self._emit(q, "op", fn, reads, writes)

    def dma(self, q, dst_ap, src_ap, reads=(), writes=()):
        self._emit(q, "dma", lambda e, d=dst_ap, s=src_ap:
                   e.dma_start(out=d, in_=s), reads, writes)

    def barrier(self):
        snap = {(q, k): self.count[(q, k)] for q in self.qs
                for k in ("op", "dma")}
        for q in self.qs:
            for (sq, sk), val in snap.items():
                if sq != q:
                    self._wait(q, sq, sk, val)

    def final_wait(self, q):
        for sq in self.qs:
            for sk in ("op", "dma"):
                self._wait(q, sq, sk, self.count[(sq, sk)])

    def enforce_dma_order(self):
        waited = {q: set() for q in self.qs}
        for q in self.qs:
            for item in self.prog[q]:
                if item[0] == "wait" and item[2] == "dma":
                    waited[item[1]].add(item[3])
        for q in self.qs:
            wl = sorted(waited[q])
            if not wl:
                continue
            out = []
            cnt = 0
            synced = 0
            import bisect
            for item in self.prog[q]:
                if item[0] == "dma":
                    i = bisect.bisect_right(wl, cnt) - 1
                    b = wl[i] if i >= 0 else 0
                    if b > synced:
                        out.append(("wait", q, "dma", b))
                        synced = b
                    cnt += 16
                    out.append(item)
                else:
                    if item[0] == "wait" and item[1] == q and \
                            item[2] == "dma":
                        synced = max(synced, item[3])
                    out.append(item)
            self.prog[q] = out

    def replay(self, q, eng, sems):
        for item in self.prog[q]:
            if item[0] == "wait":
                eng.wait_ge(sems[(item[1], item[2])], item[3])
            elif item[0] == "op":
                item[1](eng).then_inc(sems[(q, "op")], 1)
            else:
                item[1](eng).then_inc(sems[(q, "dma")], 16)



def wstage_ap(dram_ap, row0, nchunk, cols):
    t = dram_ap.tensor
    base = dram_ap.offset + row0 * cols
    return bass.AP(t, base, [[cols, 128], [128 * cols, nchunk], [1, cols]])


def build_program(consts, taps=()):
    taps = set(taps)
    nc = bass.Bass("TRN2", target_bir_lowering=False)

    ext = {}

    def EI(name, shape):
        ext[name] = nc.dram_tensor(name, shape, F32, kind="ExternalInput").ap()

    EI("x", [B_PER_CORE, DM, SEQ])
    EI("wq_t", [DM, DM]); EI("wk_t", [DM, DM]); EI("wv_t", [DM, DM])
    EI("wp_t", [DM, DM])
    EI("proj_b", [128, 8]); EI("kern3", [128, 12])
    EI("gamma_c", [128, 8]); EI("phase_c", [128, 8]); EI("theta_c", [128, 1])
    EI("gate_wt", [DM, 16])
    EI("w_gate", [16, DM, 512]); EI("w_up", [16, DM, 512])
    EI("w_down", [16, 512, DM])
    EI("ws_gate", [DM, 512]); EI("ws_up", [DM, 512]); EI("ws_down", [512, DM])
    out = nc.dram_tensor("out", [B_PER_CORE, DM, SEQ], F32,
                         kind="ExternalOutput").ap()
    dbg = {}
    if "q0" in taps:
        dbg["q0"] = nc.dram_tensor("dbg_q0", [2, 128, SEQ], F32,
                                   kind="ExternalOutput").ap()
    if "dec" in taps:
        dbg["dec_T"] = nc.dram_tensor("dbg_dec_T", [HEADS, 128, 16], F32,
                                      kind="ExternalOutput").ap()
        dbg["dec_M"] = nc.dram_tensor("dbg_dec_M", [HEADS, 128, 16], F32,
                                      kind="ExternalOutput").ap()
    if "xp1" in taps:
        dbg["xp1"] = nc.dram_tensor("dbg_xp1", [DM, SEQ], F32,
                                    kind="ExternalOutput").ap()
    if "route" in taps:
        dbg["route"] = nc.dram_tensor("dbg_route", [16, SEQ], F32,
                                      kind="ExternalOutput").ap()
    if "gain" in taps:
        dbg["gain"] = nc.dram_tensor("dbg_gain", [DM, SEQ], F32,
                                     kind="ExternalOutput").ap()
    if "lnrows" in taps:
        dbg["lnrows"] = nc.dram_tensor("dbg_lnrows", [4, SEQ], F32,
                                       kind="ExternalOutput").ap()

    cst = {k: nc.inline_tensor(np.ascontiguousarray(v), name=f"c_{k}").ap()
           for k, v in consts.items()}

    scr = {}

    def SD(name, shape, dt=F32):
        scr[name] = nc.dram_tensor(name, shape, dt).ap()

    SD("wq_bf", [DM, DM], BF16); SD("wk_bf", [DM, DM], BF16)
    SD("wv_bf", [DM, DM], BF16); SD("wp_bf", [DM, DM], BF16)
    SD("wg_bf", [16, DM, 512], BF16); SD("wu_bf", [16, DM, 512], BF16)
    SD("wd_bf", [16, 512, DM], BF16)
    SD("wsg_bf", [DM, 512], BF16); SD("wsu_bf", [DM, 512], BF16)
    SD("wsd_bf", [512, DM], BF16)
    SD("gck", [DM, SEQ], BF16); SD("gsk", [DM, SEQ], BF16)
    SD("gcq", [DM, SEQ], BF16); SD("gsq", [DM, SEQ], BF16)
    SD("mrow_d", [16, 128]); SD("trow_d", [16, 128])
    SD("rowbc", [4, SEQ])
    SD("attn_d", [DM, SEQ], BF16)

    stack = ExitStack()
    sb = {}

    def S(name, shape, dt):
        sb[name] = stack.enter_context(nc.sbuf_tensor(name, shape, dt))
        return sb[name]

    S("fsc1", [128, SEQ], F32)
    S("fsc2", [128, SEQ], F32)
    S("xchunk", [128, SEQ], F32)
    S("rows", [16, SEQ], F32)
    S("projb_t", [128, 8], F32)
    S("kern", [128, 12], F32)
    S("gammac", [128, 8], F32)
    S("phasec", [128, 8], F32)
    S("thetac", [128, 1], F32)
    for nm in ("st_tL", "st_w0", "st_tH", "st_t", "st_c", "st_cH", "st_mu",
               "st_sig", "st_negm", "st_b", "st_nb", "st_tmp", "st_tmp2",
               "st_sumd", "st_s1", "st_s2", "st_r", "st_T", "st_acc",
               "st_t2", "st_c2", "st_acc2", "st_pt",
               "st_t3", "st_c3", "st_acc3", "st_b3"):
        S(nm, [128, 16], F32)
    S("st_negm_bf", [128, 16], BF16)
    S("top16", [128, 16], F32)
    S("m8a", [128, 8], F32)
    S("m8b", [128, 8], F32)
    S("m8i", [128, 8], mybir.dt.uint32)
    S("onesb", [1, 128], BF16)
    S("onesf", [1, 128], F32)
    S("ones128", [128, 1], F32)
    S("c_iota", [128, 16], F32)
    S("c_thr", [128, 16], F32)
    S("c_zlo", [128, 16], F32)
    S("c_zhi", [128, 16], F32)
    S("c_nan", [128, 16], F32)
    S("c_it0", [128, 16], F32)
    S("c_dm", [128, 16], F32)
    S("c_do", [128, 16], F32)
    S("c_id", [128, 128], F32)
    RQK = S("RQK", [128, 32768], BF16)
    RVT = S("RVT", [128, 16896], BF16)
    RU = S("RU", [128, 28672], BF16)

    def V(region, byte_off, parts, cols, dt):
        if dt == F32:
            assert byte_off % 4 == 0
            w = cols * 2
        else:
            assert byte_off % 2 == 0
            w = cols
        c0 = byte_off // 2
        ap = region[0:parts, c0:c0 + w]
        if dt != BF16:
            ap = ap.bitcast(dt)
        return ap

    QK = [V(RQK, f * 4096, 128, SEQ, BF16) for f in range(8)]
    KK = [V(RQK, 32768 + f * 4096, 128, SEQ, BF16) for f in range(8)]
    XP1 = [V(RQK, f * 8192, 128, SEQ, F32) for f in range(8)]
    VT = V(RVT, 0, 128, NT * VBLK, BF16)
    LN2 = [V(RVT, c * 4096, 128, SEQ, BF16) for c in range(8)]
    GSC = V(RU, 0, 128, SEQ, F32)
    GSC2 = V(RU, 8192, 128, SEQ, F32)
    GBF = V(RU, 16384, 128, SEQ, BF16)
    GARG = V(RU, 20480, 128, SEQ, F32)
    TROWT = V(RU, 28672, 128, SEQ, F32)
    RC2T = V(RU, 36864, 128, SEQ, F32)
    RS2T = V(RU, 45056, 128, SEQ, F32)
    XD = [V(RU, c * 4096, 128, SEQ, BF16) for c in range(8)]
    WFULL = V(RU, 32768, 128, 2 * DM, BF16)
    GT = V(RU, 36864, 128, SEQ, BF16)
    MUBC = V(RU, 40960, 128, SEQ, F32)
    RSTDBC = V(RU, 49152, 128, SEQ, F32)
    S16 = V(RU, 0, 128, 5888, F16)
    DSB = V(RU, 11776, 128, 5888, F32)
    JUNK16 = V(RU, 35328, 128, SEQ, F16)
    TBC = V(RU, 39424, 128, 512, F32)
    EBF = V(RU, 41472, 128, 512, BF16)
    INDBF = V(RU, 42496, 128, 512, BF16)
    PBF = V(RU, 43520, 128, 512, BF16)
    ABF1 = V(RU, 44544, 64, 512, BF16)
    ATTBF = V(RU, 45568, 128, 512, BF16)
    ATTBF2 = V(RU, 46592 + 2048, 128, 512, BF16)
    RZBC = V(RU, 46592, 64, 512, F32)
    INDBF2 = V(RU, 48640, 128, 512, BF16)
    PBF2 = V(RU, 49664, 128, 512, BF16)
    EBF2 = V(RU, 55808, 128, 512, BF16)
    CCA = [V(RU, 50688 + oi * 1024, 128, 512, BF16) for oi in range(4)]
    SC_DIAG = V(RU, 54784, 128, 128, F32)
    C_DIAG = V(RU, 55296, 128, 128, F32)
    SU = [V(RU, m * 4096, 128, SEQ, BF16) for m in range(4)]
    MUBC2 = V(RU, 16384, 128, SEQ, F32)
    RSTDBC2 = V(RU, 24576, 128, SEQ, F32)
    WSTG = V(RU, 16384, 128, 8 * 512, BF16)
    WSTU = V(RU, 24576, 128, 8 * 512, BF16)
    WSTD = V(RU, 32768, 128, 4 * DM, BF16)
    GATEW = V(RU, 40960, 128, 128, F32)
    WEBC = V(RU, 41472, 128, SEQ, F32)
    BSC1 = sb["xchunk"][:, 0:1024].bitcast(BF16)
    BSC2 = sb["xchunk"][:, 1024:2048].bitcast(BF16)

    ps1 = stack.enter_context(nc.psum_tensor("ps1", [128, SEQ], F32))
    ps2 = stack.enter_context(nc.psum_tensor("ps2", [128, SEQ], F32))
    PS1B = ["ps1b0", "ps1b1", "ps1b2", "ps1b3"]
    PS2B = ["ps2b0", "ps2b1", "ps2b2", "ps2b3"]

    def p1b(c0, c1):
        return [PS1B[i] for i in range(c0 // 512, (c1 + 511) // 512)]

    def p2b(c0, c1):
        return [PS2B[i] for i in range(c0 // 512, (c1 + 511) // 512)]

    ctx = Ctx()
    for qn in ("sp", "vec", "scl", "ten", "pol"):
        ctx.add_queue(qn)
    OP = ctx.op
    DMA = ctx.dma

    for dst, src, res in (
            (sb["c_iota"][:], cst["iota16"], "c_iota"),
            (sb["c_thr"][:], cst["cthr"], "c_thr"),
            (sb["c_zlo"][:], cst["z_lo"], "c_zlo"),
            (sb["c_zhi"][:], cst["z_hi"], "c_zhi"),
            (sb["c_nan"][:], cst["neg_an_inv"], "c_nan"),
            (sb["c_it0"][:], cst["inv_T0"], "c_it0"),
            (sb["c_dm"][:], cst["decmask"], "c_dm"),
            (sb["c_do"][:], cst["decoff"], "c_do"),
            (sb["c_id"][:], cst["ident"], "c_id"),
            (sb["projb_t"][:], ext["proj_b"], "projb_t"),
            (sb["kern"][:], ext["kern3"], "kern"),
            (sb["gammac"][:], ext["gamma_c"], "gammac"),
            (sb["phasec"][:], ext["phase_c"], "phasec"),
            (sb["thetac"][:], ext["theta_c"], "thetac"),
            (TROWT, cst["trow"], "trowt"),
            (RC2T, cst["rope_c2"], "rc2t"),
            (RS2T, cst["rope_s2"], "rs2t")):
        DMA("sp", dst, src, [], [res])
    OP("vec", lambda e: e.memset(sb["onesb"][:], 1.0), [], ["onesb"])
    OP("vec", lambda e: e.memset(sb["onesf"][:], 1.0), [], ["onesf"])
    OP("vec", lambda e: e.memset(sb["ones128"][:], 1.0), [], ["ones128"])
    OP("vec", lambda e: e.memset(sb["st_c"][:], 0.0), [], ["st_c"])
    OP("vec", lambda e: e.memset(sb["st_c2"][:], 0.0), [], ["st_c2"])
    OP("vec", lambda e: e.memset(sb["st_c3"][:], 0.0), [], ["st_c3"])

    def convert_weights():
        jobs = [(ext["wq_t"], scr["wq_bf"], DM, DM),
                (ext["wk_t"], scr["wk_bf"], DM, DM),
                (ext["wv_t"], scr["wv_bf"], DM, DM),
                (ext["wp_t"], scr["wp_bf"], DM, DM),
                (ext["ws_gate"], scr["wsg_bf"], DM, 512),
                (ext["ws_up"], scr["wsu_bf"], DM, 512),
                (ext["ws_down"], scr["wsd_bf"], 512, DM)]
        for e_i in range(16):
            jobs.append((ext["w_gate"][e_i], scr["wg_bf"][e_i], DM, 512))
            jobs.append((ext["w_up"][e_i], scr["wu_bf"][e_i], DM, 512))
            jobs.append((ext["w_down"][e_i], scr["wd_bf"][e_i], 512, DM))
        ci = 0
        for src, dst, rows_, cols in jobs:
            for r0 in range(0, rows_, 128):
                hf = ci % 2
                ci += 1
                i0 = hf * 1024
                DMA("sp", sb["fsc1"][:, i0:i0 + cols], src[r0:r0 + 128, :],
                    [], [f"f1{hf}"])
                OP("vec", lambda e, c=cols, x0=i0, h_=hf: e.tensor_copy(
                    out=sb["fsc2"].bitcast(BF16)[:, x0:x0 + c],
                    in_=sb["fsc1"][:, x0:x0 + c]),
                   reads=[f"f1{hf}"], writes=[f"f2{hf}"])
                DMA("pol", dst[r0:r0 + 128, :],
                    sb["fsc2"].bitcast(BF16)[:, i0:i0 + cols],
                    [f"f2{hf}"], [])
        OP("vec", lambda e: e.memset(sb["fsc1"][0:1, 0:1], 0.0),
           reads=["f10", "f11", "f20", "f21"], writes=["fsc1", "fsc2"])

    if "nop0" not in taps:
        convert_weights()

    def build_gain():
        INV2PI = float(np.float32(1.0 / (2 * np.pi)))
        C1 = 6.28125
        C2c = 1.9353071599174291e-03
        C3c = 1.1280587064619362e-08
        MAGIC = 8388608.0
        HALFPI = math.pi / 2
        TWO_PI32 = float(np.float32(2 * np.pi))

        def cos_inplace(arg_ap, arg_res, dst_ap, dst_res):
            OP("vec", lambda e: e.tensor_scalar(
                out=sb["xchunk"][:], in0=arg_ap, scalar1=HALFPI,
                scalar2=INV2PI, op0=ALU.add, op1=ALU.mult),
               reads=[arg_res], writes=["xchunk"])
            OP("vec", lambda e: e.tensor_scalar(
                out=sb["xchunk"][:], in0=sb["xchunk"][:], scalar1=MAGIC,
                scalar2=-MAGIC, op0=ALU.add, op1=ALU.add),
               reads=["xchunk"], writes=["xchunk"])
            OP("vec", lambda e: e.tensor_scalar(
                out=arg_ap, in0=arg_ap, scalar1=HALFPI, scalar2=None,
                op0=ALU.add), reads=[arg_res], writes=[arg_res])
            for Cc in (C1, C2c, C3c):
                OP("vec", lambda e, C=Cc: e.scalar_tensor_tensor(
                    out=arg_ap, in0=sb["xchunk"][:], scalar=-C, in1=arg_ap,
                    op0=ALU.mult, op1=ALU.add),
                   reads=["xchunk", arg_res], writes=[arg_res])
            OP("vec", lambda e: e.tensor_scalar(
                out=arg_ap, in0=arg_ap, scalar1=-3.141592653589793,
                scalar2=3.141592653589793, op0=ALU.max, op1=ALU.min),
               reads=[arg_res], writes=[arg_res])
            OP("scl", lambda e: e.activation(
                out=dst_ap, in_=arg_ap, func=AF.Sin, bias=0.0, scale=1.0),
               reads=[arg_res], writes=[dst_res])

        for ci in range(8):
            OP("vec", lambda e, c=ci: e.tensor_scalar(
                out=sb["st_s1"][:, 0:1], in0=sb["gammac"][:, c:c + 1],
                scalar1=TWO_PI32, scalar2=None, op0=ALU.mult),
               reads=["gammac"], writes=["st_s1"])
            OP("vec", lambda e: e.tensor_scalar(
                out=GARG, in0=TROWT, scalar1=sb["st_s1"][:, 0:1],
                scalar2=0.01, op0=ALU.mult, op1=ALU.mult),
               reads=["trowt", "st_s1"], writes=["garg"])
            OP("vec", lambda e, c=ci: e.tensor_scalar(
                out=GARG, in0=GARG, scalar1=sb["phasec"][:, c:c + 1],
                scalar2=None, op0=ALU.add),
               reads=["garg", "phasec"], writes=["garg"])
            cos_inplace(GARG, "garg", GSC, "gsc")
            OP("vec", lambda e: e.tensor_scalar(
                out=GSC, in0=GSC, scalar1=0.3, scalar2=0.5, op0=ALU.mult,
                op1=ALU.add), reads=["gsc"], writes=["gsc"])
            OP("vec", lambda e: e.tensor_scalar(
                out=sb["st_s1"][:, 1:2], in0=sb["thetac"][:, 0:1],
                scalar1=TWO_PI32, scalar2=None, op0=ALU.mult),
               reads=["thetac"], writes=["st_s1"])
            OP("vec", lambda e: e.tensor_scalar(
                out=GARG, in0=TROWT, scalar1=sb["st_s1"][:, 1:2],
                scalar2=0.01, op0=ALU.mult, op1=ALU.mult),
               reads=["trowt", "st_s1"], writes=["garg"])
            cos_inplace(GARG, "garg", GSC2, "gsc2")
            OP("vec", lambda e: e.scalar_tensor_tensor(
                out=GSC, in0=GSC2, scalar=0.2, in1=GSC, op0=ALU.mult,
                op1=ALU.add), reads=["gsc", "gsc2"], writes=["gsc"])
            if "gain" in taps:
                DMA("pol", dbg["gain"][ci * 128:(ci + 1) * 128, :], GSC,
                    ["gsc"], [])
            OP("vec", lambda e: e.stream_shuffle(
                out=GSC2, in_=GSC, mask=SWAPMASK),
               reads=["gsc"], writes=["gsc2"])
            OP("vec", lambda e: e.tensor_tensor(
                out=GBF, in0=GSC, in1=RC2T, op=ALU.mult),
               reads=["gsc", "rc2t"], writes=["gbf"])
            DMA("pol", scr["gck"][ci * 128:(ci + 1) * 128, :], GBF,
                ["gbf"], [])
            OP("vec", lambda e: e.tensor_tensor(
                out=GBF, in0=GSC2, in1=RS2T, op=ALU.mult),
               reads=["gsc2", "rs2t"], writes=["gbf"])
            DMA("pol", scr["gsk"][ci * 128:(ci + 1) * 128, :], GBF,
                ["gbf"], [])
            OP("vec", lambda e: e.tensor_scalar(
                out=GSC, in0=GSC, scalar1=0.125, scalar2=None, op0=ALU.mult),
               reads=["gsc"], writes=["gsc"])
            OP("vec", lambda e: e.tensor_scalar(
                out=GSC2, in0=GSC2, scalar1=0.125, scalar2=None,
                op0=ALU.mult), reads=["gsc2"], writes=["gsc2"])
            OP("vec", lambda e: e.tensor_tensor(
                out=GBF, in0=GSC, in1=RC2T, op=ALU.mult),
               reads=["gsc", "rc2t"], writes=["gbf"])
            DMA("pol", scr["gcq"][ci * 128:(ci + 1) * 128, :], GBF,
                ["gbf"], [])
            OP("vec", lambda e: e.tensor_tensor(
                out=GBF, in0=GSC2, in1=RS2T, op=ALU.mult),
               reads=["gsc2", "rs2t"], writes=["gbf"])
            DMA("pol", scr["gsq"][ci * 128:(ci + 1) * 128, :], GBF,
                ["gbf"], [])

    if "nop0" not in taps:
        build_gain()
    ctx.barrier()

    def load_x_chunk(b, ci):
        DMA("sp", sb["xchunk"][:], ext["x"][b, ci * 128:(ci + 1) * 128, :],
            [], ["xchunk"])
        return sb["xchunk"][:]

    def ln_stats(src_fn, mu_ap, rstd_ap):
        for ci in range(8):
            ap, res = src_fn(ci)
            OP("scl", lambda e, a=ap: e.activation(
                out=sb["fsc1"][:], in_=a, func=AF.Square, bias=0.0,
                scale=1.0), reads=[res], writes=["fsc1"])
            for nbk in range(4):
                s0, s1 = nbk * 512, nbk * 512 + 512
                OP("ten", lambda e, a=ap, x0=s0, x1=s1, c=ci, r=res: MM(
                    e, ps2[0:1, x0:x1], sb["ones128"][:], a[:, x0:x1],
                    start=(c == 0), stop=(c == 7)),
                   reads=[res, "ones128"], writes=p2b(s0, s1))
                OP("ten", lambda e, x0=s0, x1=s1, c=ci: MM(
                    e, ps2[32:33, x0:x1], sb["ones128"][:], sb["fsc1"][:, x0:x1],
                    start=(c == 0), stop=(c == 7)),
                   reads=["fsc1", "ones128"], writes=p2b(s0, s1))
        OP("scl", lambda e: e.activation(
            out=sb["rows"][0:1, :], in_=ps2[0:1, :], func=AF.Identity,
            bias=0.0, scale=float(1.0 / DM)), reads=PS2B, writes=["rows"])
        OP("scl", lambda e: e.activation(
            out=sb["fsc1"][0:1, :], in_=ps2[32:33, :], func=AF.Identity,
            bias=0.0, scale=float(1.0 / DM)), reads=PS2B, writes=["fsc1"])
        OP("vec", lambda e: e.tensor_tensor(
            out=sb["fsc2"][0:1, :], in0=sb["rows"][0:1, :],
            in1=sb["rows"][0:1, :], op=ALU.mult),
           reads=["rows"], writes=["fsc2"])
        OP("vec", lambda e: e.tensor_tensor(
            out=sb["fsc2"][0:1, :], in0=sb["fsc1"][0:1, :],
            in1=sb["fsc2"][0:1, :], op=ALU.subtract),
           reads=["fsc1", "fsc2"], writes=["fsc2"])
        OP("vec", lambda e: e.tensor_scalar(
            out=sb["fsc2"][0:1, :], in0=sb["fsc2"][0:1, :], scalar1=1e-5,
            scalar2=None, op0=ALU.add), reads=["fsc2"], writes=["fsc2"])
        if "lnrows" in taps:
            DMA("pol", dbg["lnrows"][0:1, :], sb["rows"][0:1, :],
                ["rows"], [])
            DMA("pol", dbg["lnrows"][1:2, :], sb["fsc1"][0:1, :],
                ["fsc1"], [])
            DMA("pol", dbg["lnrows"][2:3, :], sb["fsc2"][0:1, :],
                ["fsc2"], [])
        OP("scl", lambda e: e.activation(
            out=sb["fsc1"][0:1, :], in_=sb["fsc2"][0:1, :], func=AF.Sqrt,
            bias=0.0, scale=1.0), reads=["fsc2"], writes=["fsc1"])
        OP("vec", lambda e: e.reciprocal(
            out=sb["fsc2"][0:1, :], in_=sb["fsc1"][0:1, :]),
           reads=["fsc1"], writes=["fsc2"])
        DMA("pol", scr["rowbc"][0:1, :], sb["rows"][0:1, :],
            ["rows"], ["rb0"])
        DMA("pol", scr["rowbc"][1:2, :], sb["fsc2"][0:1, :],
            ["fsc2"], ["rb1"])
        DMA("sp", mu_ap, scr["rowbc"][0:1, :].to_broadcast([128, SEQ]),
            ["rb0"], ["mubc"])
        DMA("sp", rstd_ap, scr["rowbc"][1:2, :].to_broadcast([128, SEQ]),
            ["rb1"], ["rstdbc"])

    def projections(b):
        ln_stats(lambda ci: (load_x_chunk(b, ci), "xchunk"), MUBC, RSTDBC)
        if "statsonly" in taps:
            return

        for tau, wsrc, kcol in (("v", scr["wv_bf"], 8),
                                ("k", scr["wk_bf"], 4),
                                ("q", scr["wq_bf"], 0)):
            for ci in range(8):
                load_x_chunk(b, ci)
                OP("vec", lambda e: e.tensor_tensor(
                    out=sb["fsc1"][:], in0=sb["xchunk"][:], in1=MUBC,
                    op=ALU.subtract),
                   reads=["xchunk", "mubc"], writes=["fsc1"])
                OP("vec", lambda e: e.tensor_tensor(
                    out=sb["fsc2"][:], in0=sb["fsc1"][:], in1=RSTDBC,
                    op=ALU.mult), reads=["fsc1", "rstdbc"], writes=["fsc2"])
                OP("scl", lambda e, kc=kcol: e.activation(
                    out=sb["fsc1"][:], in_=sb["fsc2"][:], func=AF.Identity,
                    bias=0.0, scale=sb["kern"][:, kc + 3:kc + 4]),
                   reads=["fsc2", "kern"], writes=["fsc1"])
                for sh in (1, 2, 3):
                    OP("vec", lambda e, s=sh, kc=kcol: e.scalar_tensor_tensor(
                        out=sb["fsc1"][:, s:SEQ],
                        in0=sb["fsc2"][:, 0:SEQ - s],
                        scalar=sb["kern"][:, kc + 3 - s:kc + 4 - s],
                        in1=sb["fsc1"][:, s:SEQ], op0=ALU.mult, op1=ALU.add),
                       reads=["fsc2", "kern", "fsc1"], writes=["fsc1"])
                OP("pol", lambda e, c=ci: e.tensor_copy(
                    out=XD[c], in_=sb["fsc1"][:]),
                   reads=["fsc1"], writes=[f"xd{ci}"])
            if tau == "v":
                for tt in range(NT):
                    for cpair in range(4):
                        DMA("sp", WFULL,
                            wstage_ap(wsrc, cpair * 256, 2, DM), [], ["wfull"])
                        for cc in range(2):
                            c = cpair * 2 + cc
                            for hf in range(2):
                                s0, s1 = hf * 512, hf * 512 + 512
                                OP("ten",
                                   lambda e, t=tt, ci2=c, c2=cc, x0=s0, x1=s1:
                                   MM(e, ps1[:, x0:x1],
                                      XD[ci2][:, t * 128:t * 128 + 128],
                                      WFULL[:, c2 * DM + x0:c2 * DM + x1],
                                      start=(ci2 == 0), stop=(ci2 == 7)),
                                   reads=[f"xd{c}", "wfull"],
                                   writes=p1b(s0, s1))
                    OP("vec", lambda e, t=tt: e.tensor_copy(
                        out=bass.AP(sb["RVT"], t * VBLK,
                                    [[16896, 128], [VSTRIDE, 16], [1, 64]]),
                        in_=bass.AP(ps1, 0,
                                    [[2048, 128], [64, 16], [1, 64]])),
                       reads=PS1B[:2], writes=["vt"])
                continue
            dst = QK if tau == "q" else KK
            dres = "q" if tau == "q" else "k"
            gc = scr["gcq"] if tau == "q" else scr["gck"]
            gs = scr["gsq"] if tau == "q" else scr["gsk"]
            for f in range(8):
                for cpair in range(4):
                    DMA("sp", WFULL,
                        wstage_ap(wsrc, cpair * 256, 2, DM), [], ["wfull"])
                    for cc in range(2):
                        c = cpair * 2 + cc
                        for nbk in range(4):
                            s0, s1 = nbk * 512, nbk * 512 + 512
                            OP("ten",
                               lambda e, ff=f, ci2=c, c2=cc, x0=s0, x1=s1:
                               MM(e, ps1[:, x0:x1],
                                  WFULL[:, c2 * DM + ff * 128:
                                        c2 * DM + ff * 128 + 128],
                                  XD[ci2][:, x0:x1],
                                  start=(ci2 == 0), stop=(ci2 == 7)),
                               reads=[f"xd{c}", "wfull"],
                               writes=p1b(s0, s1))
                OP("vec", lambda e: e.tensor_copy(out=BSC1, in_=ps1[:]),
                   reads=PS1B, writes=["xchunk"])
                OP("vec", lambda e: e.stream_shuffle(
                    out=BSC2, in_=BSC1, mask=SWAPMASK),
                   reads=["xchunk"], writes=["xchunk"])
                DMA("sp", GT, gc[f * 128:(f + 1) * 128, :], [], ["gt"])
                OP("vec", lambda e, d=dst[f], r=f"{dres}{f}": e.tensor_tensor(
                    out=d, in0=BSC1, in1=GT, op=ALU.mult),
                   reads=["xchunk", "gt"], writes=[f"{dres}{f}"])
                DMA("sp", GT, gs[f * 128:(f + 1) * 128, :], [], ["gt"])
                OP("vec", lambda e: e.tensor_tensor(
                    out=BSC2, in0=BSC2, in1=GT, op=ALU.mult),
                   reads=["xchunk", "gt"], writes=["xchunk"])
                OP("vec", lambda e, d=dst[f], r=f"{dres}{f}": e.tensor_tensor(
                    out=d, in0=d, in1=BSC2, op=ALU.add),
                   reads=[f"{dres}{f}", "xchunk"], writes=[f"{dres}{f}"])
        OP("vec", lambda e: e.memset(
            bass.AP(sb["RVT"], 64, [[16896, 128], [VBLK, 16],
                                    [VSTRIDE, 16], [1, 1]]), 1.0),
           reads=["vt"], writes=["vt_ones"])
        if "q0" in taps and b == 0:
            OP("vec", lambda e: e.tensor_copy(out=sb["fsc1"][:], in_=QK[0]),
               reads=["q0"], writes=["fsc1"])
            DMA("pol", dbg["q0"][0], sb["fsc1"][:], ["fsc1"], [])
            OP("vec", lambda e: e.tensor_copy(out=sb["fsc1"][:], in_=KK[0]),
               reads=["k0"], writes=["fsc1"])
            DMA("pol", dbg["q0"][1], sb["fsc1"][:], ["fsc1"], [])

    def joff(T, g0):
        off = 0
        for t in range(g0, T):
            off += 128 * (t + 1)
        return off

    def probe_counts(g0, tiles, pr=0):
        tnm = ("st_t", "st_t2", "st_t3")[pr]
        cnm = ("st_c", "st_c2", "st_c3")[pr]
        anm = ("st_acc", "st_acc2", "st_acc3")[pr]
        for T in tiles:
            if T not in ACT_TILES:
                continue
            jW = 128 * (T + 1)
            off = joff(T, g0)
            OP("scl", lambda e, o=off, w=jW, T_=T, tn=tnm, an=anm:
               e.activation(
                   out=ps1[:, 0:w], in_=S16[:, o:o + w], func=AF.Sign,
                   bias=sb[tn][:, T_:T_ + 1], scale=-1.0,
                   accum_out=sb[an][:, T_:T_ + 1]),
               reads=["s16", tnm], writes=p1b(0, jW) + [anm])
        for T in tiles:
            if T in ACT_TILES:
                continue
            jW = 128 * (T + 1)
            off = joff(T, g0)
            OP("vec", lambda e, o=off, w=jW, T_=T, tn=tnm, cn=cnm:
               e.tensor_scalar(
                   out=JUNK16[:, 0:w], in0=S16[:, o:o + w],
                   scalar1=sb[tn][:, T_:T_ + 1], scalar2=None,
                   op0=ALU.is_ge, op1=ALU.add,
                   accum_out=sb[cn][:, T_:T_ + 1]),
               reads=["s16", tnm], writes=["junk16", cnm])

    def gate(bnm, dsl, asl, cnm, anm):
        OP("vec", lambda e, d=dsl, bn=bnm, cn=cnm: e.tensor_scalar(
            out=sb[bn][:, d], in0=sb[cn][:, d], scalar1=TARGET,
            scalar2=None, op0=ALU.is_lt), reads=[cnm], writes=[bnm])
        OP("vec", lambda e, a=asl, bn=bnm, an=anm: e.tensor_tensor(
            out=sb[bn][:, a], in0=sb[an][:, a], in1=sb["c_thr"][:, a],
            op=ALU.is_ge), reads=[anm, "c_thr"], writes=[bnm])

    def zupdate(g, dsl, asl, pr=0):
        tnm = "st_t" if pr == 0 else "st_t2"
        cnm = "st_c" if pr == 0 else "st_c2"
        anm = "st_acc" if pr == 0 else "st_acc2"
        gate("st_b", dsl, asl, cnm, anm)
        OP("vec", lambda e: e.tensor_scalar(
            out=sb["st_nb"][:, g], in0=sb["st_b"][:, g], scalar1=-1.0,
            scalar2=1.0, op0=ALU.mult, op1=ALU.add),
           reads=["st_b"], writes=["st_nb"])
        for (dst, src, gt_) in (("st_tH", tnm, "st_b"),
                                ("st_cH", cnm, "st_b"),
                                ("st_tL", tnm, "st_nb")):
            OP("vec", lambda e, d=dst, s=src: e.tensor_tensor(
                out=sb["st_tmp"][:, g], in0=sb[s][:, g], in1=sb[d][:, g],
                op=ALU.subtract), reads=[src, dst], writes=["st_tmp"])
            OP("vec", lambda e, gt=gt_: e.tensor_tensor(
                out=sb["st_tmp"][:, g], in0=sb[gt][:, g],
                in1=sb["st_tmp"][:, g], op=ALU.mult),
               reads=[gt_, "st_tmp"], writes=["st_tmp"])
            OP("vec", lambda e, d=dst: e.tensor_tensor(
                out=sb[d][:, g], in0=sb[d][:, g], in1=sb["st_tmp"][:, g],
                op=ALU.add), reads=[dst, "st_tmp"], writes=[dst])

    def bupdate(g, dsl, asl, lv):
        pr = lv % 3
        tA = ("st_t", "st_t2", "st_t3")[pr]
        tB = ("st_t", "st_t2", "st_t3")[(lv + 1) % 3]
        cA = ("st_c", "st_c2", "st_c3")[pr]
        aA = ("st_acc", "st_acc2", "st_acc3")[pr]
        bn = ("st_b", "st_nb", "st_b3")[pr]
        gate(bn, dsl, asl, cA, aA)
        a = float(2.0 ** (-(lv + 2)))
        OP("vec", lambda e, b_=bn, s1=-2.0 * a, s2=a: e.tensor_scalar(
            out=sb["st_tmp"][:, g], in0=sb[b_][:, g], scalar1=s1,
            scalar2=s2, op0=ALU.mult, op1=ALU.add),
           reads=[bn], writes=["st_tmp"])
        OP("vec", lambda e: e.tensor_tensor(
            out=sb["st_tmp"][:, g], in0=sb["st_tmp"][:, g],
            in1=sb["st_w0"][:, g], op=ALU.mult),
           reads=["st_tmp", "st_w0"], writes=["st_tmp"])
        OP("vec", lambda e, src_=tA, dst_=tB: e.tensor_tensor(
            out=sb[dst_][:, g], in0=sb[src_][:, g], in1=sb["st_tmp"][:, g],
            op=ALU.add), reads=[tA, "st_tmp"], writes=[tB])
        for (dst, s_) in (("st_tH", tA), ("st_cH", cA)):
            OP("pol", lambda e, d=dst, s=s_: e.tensor_tensor(
                out=sb["st_pt"][:, g], in0=sb[s][:, g], in1=sb[d][:, g],
                op=ALU.subtract), reads=[s_, dst], writes=["st_pt"])
            OP("pol", lambda e, b_=bn: e.tensor_tensor(
                out=sb["st_pt"][:, g], in0=sb[b_][:, g],
                in1=sb["st_pt"][:, g], op=ALU.mult),
               reads=[bn, "st_pt"], writes=["st_pt"])
            OP("pol", lambda e, d=dst: e.tensor_tensor(
                out=sb[d][:, g], in0=sb[d][:, g], in1=sb["st_pt"][:, g],
                op=ALU.add), reads=[dst, "st_pt"], writes=[dst])

    def extract(T, g0):
        jW = 128 * (T + 1)
        off = joff(T, g0)
        OP("vec", lambda e, o=off, w=jW, T_=T: e.tensor_scalar(
            out=sb["fsc1"][:, 0:w], in0=S16[:, o:o + w],
            scalar1=sb["st_tH"][:, T_:T_ + 1], scalar2=-1e30,
            op0=ALU.is_ge, op1=ALU.mult),
           reads=["s16", "st_tH"], writes=["fsc1"])
        OP("vec", lambda e, o=off, w=jW: e.tensor_tensor(
            out=sb["fsc2"][:, 0:w], in0=sb["fsc1"][:, 0:w],
            in1=DSB[:, o:o + w], op=ALU.add),
           reads=["fsc1", "dsb"], writes=["fsc2"])
        OP("vec", lambda e, w=jW: e.max(out=sb["m8a"][:],
                                        in_=sb["fsc2"][:, 0:w]),
           reads=["fsc2"], writes=["m8a"])
        OP("vec", lambda e, w=jW: e.match_replace(
            out=sb["fsc1"][:, 0:w], in_to_replace=sb["m8a"][:],
            in_values=sb["fsc2"][:, 0:w], imm_value=-1e30),
           reads=["m8a", "fsc2"], writes=["fsc1"])
        OP("vec", lambda e, w=jW: e.max(out=sb["m8b"][:],
                                        in_=sb["fsc1"][:, 0:w]),
           reads=["fsc1"], writes=["m8b"])
        OP("vec", lambda e: e.tensor_copy(out=sb["top16"][:, 0:8],
                                          in_=sb["m8a"][:]),
           reads=["m8a"], writes=["top16"])
        OP("vec", lambda e: e.tensor_copy(out=sb["top16"][:, 8:16],
                                          in_=sb["m8b"][:]),
           reads=["m8b"], writes=["top16"])
        OP("vec", lambda e, T_=T: e.tensor_scalar(
            out=sb["st_tmp"][:], in0=sb["c_iota"][:],
            scalar1=sb["st_r"][:, T_:T_ + 1], scalar2=None,
            op0=ALU.is_equal),
           reads=["c_iota", "st_r"], writes=["st_tmp"])
        OP("vec", lambda e: e.tensor_tensor(
            out=sb["st_tmp"][:], in0=sb["st_tmp"][:], in1=sb["top16"][:],
            op=ALU.mult), reads=["st_tmp", "top16"], writes=["st_tmp"])
        OP("vec", lambda e, T_=T: e.tensor_reduce(
            out=sb["st_T"][:, T_:T_ + 1], in_=sb["st_tmp"][:], axis=AX,
            op=ALU.add), reads=["st_tmp"], writes=["st_T"])

    def decision(b, h):
        f = h // 2
        r0 = (h % 2) * 64
        tp = (64, 0) if r0 else None
        qt, kt = QK[f], KK[f]
        qr, kr = f"q{f}", f"k{f}"

        OP("ten", lambda e: MM(
            e, ps1[:, 0:128], qt[r0:r0 + 64, 0:128], kt[r0:r0 + 64, 0:128],
            start=True, stop=True, tile_position=tp),
           reads=[qr, kr], writes=[PS1B[0]])
        OP("vec", lambda e: e.tensor_tensor(
            out=SC_DIAG, in0=ps1[:, 0:128], in1=C_DIAG, op=ALU.add),
           reads=[PS1B[0], "c_diag"], writes=["sc_diag"])
        OP("vec", lambda e: e.tensor_reduce(
            out=sb["st_negm"][:, 0:1], in_=SC_DIAG, axis=AX,
            op=ALU.max, negate=True), reads=["sc_diag"], writes=["st_negm"])
        OP("vec", lambda e: e.tensor_copy(
            out=sb["st_negm_bf"][:, 0:1], in_=sb["st_negm"][:, 0:1]),
           reads=["st_negm"], writes=["st_negm_bf"])
        OP("vec", lambda e: e.tensor_copy(
            out=sb["st_negm"][:, 0:1], in_=sb["st_negm_bf"][:, 0:1]),
           reads=["st_negm_bf"], writes=["st_negm"])
        OP("vec", lambda e: e.memset(sb["st_T"][:, 0:1], KEEPALL_T),
           [], ["st_T"])

        for (g0, g1) in GRP:
            tiles = list(range(g0, g1 + 1))
            for T in tiles:
                jW = 128 * (T + 1)
                T0 = 128 * T
                off = joff(T, g0)
                for jc in range(0, jW, 512):
                    w = min(512, jW - jc)
                    OP("ten", lambda e, TT=T, j=jc, ww=w: MM(
                        e, ps1[:, j:j + ww],
                        qt[r0:r0 + 64, TT * 128:TT * 128 + 128],
                        kt[r0:r0 + 64, j:j + ww], start=True, stop=True,
                        tile_position=tp),
                       reads=[qr, kr], writes=p1b(jc, jc + w))
                OP("vec", lambda e, T0_=T0: e.tensor_tensor(
                    out=SC_DIAG, in0=ps1[:, T0_:T0_ + 128], in1=C_DIAG,
                    op=ALU.add),
                   reads=p1b(T0, T0 + 128) + ["c_diag"], writes=["sc_diag"])
                OP("vec", lambda e, T0_=T0: e.tensor_reduce(
                    out=sb["st_s1"][:, 0:1], in_=ps1[:, 0:T0_], axis=AX,
                    op=ALU.max, negate=True),
                   reads=p1b(0, T0), writes=["st_s1"])
                OP("vec", lambda e: e.tensor_reduce(
                    out=sb["st_s2"][:, 0:1], in_=SC_DIAG, axis=AX,
                    op=ALU.max, negate=True),
                   reads=["sc_diag"], writes=["st_s2"])
                OP("vec", lambda e, T_=T: e.tensor_tensor(
                    out=sb["st_negm_bf"][:, T_:T_ + 1],
                    in0=sb["st_s1"][:, 0:1],
                    in1=sb["st_s2"][:, 0:1], op=ALU.min),
                   reads=["st_s1", "st_s2"], writes=["st_negm_bf"])
                OP("vec", lambda e, T_=T: e.tensor_copy(
                    out=sb["st_negm"][:, T_:T_ + 1],
                    in_=sb["st_negm_bf"][:, T_:T_ + 1]),
                   reads=["st_negm_bf"], writes=["st_negm"])
                OP("scl", lambda e, T_=T, T0_=T0, o=off: e.activation(
                    out=DSB[:, o:o + T0_], in_=ps1[:, 0:T0_],
                    func=AF.Identity, bias=sb["st_negm"][:, T_:T_ + 1],
                    scale=1.0, accum_out=sb["st_sumd"][:, T_:T_ + 1]),
                   reads=p1b(0, T0) + ["st_negm"],
                   writes=["dsb", "st_sumd"])
                OP("scl", lambda e, T_=T, T0_=T0, o=off, w=jW: e.activation(
                    out=DSB[:, o + T0_:o + w], in_=SC_DIAG,
                    func=AF.Identity, bias=sb["st_negm"][:, T_:T_ + 1],
                    scale=1.0), reads=["sc_diag", "st_negm"], writes=["dsb"])
                OP("scl", lambda e, o=off, w=jW: e.activation(
                    out=S16[:, o:o + w], in_=DSB[:, o:o + w],
                    func=AF.Identity, bias=0.0, scale=1.0),
                   reads=["dsb"], writes=["s16"])
                OP("vec", lambda e, T_=T: e.tensor_scalar(
                    out=sb["st_mu"][:, T_:T_ + 1],
                    in0=sb["st_sumd"][:, T_:T_ + 1],
                    scalar1=sb["c_it0"][:, T_:T_ + 1], scalar2=None,
                    op0=ALU.mult),
                   reads=["st_sumd", "c_it0"], writes=["st_mu"])

            g = slice(g0, g1 + 1)
            dsl, asl = GSL[g0]
            OP("vec", lambda e, gg=g: e.tensor_tensor(
                out=sb["st_sig"][:, gg], in0=sb["st_mu"][:, gg],
                in1=sb["c_nan"][:, gg], op=ALU.mult),
               reads=["st_mu", "c_nan"], writes=["st_sig"])
            OP("vec", lambda e, gg=g: e.scalar_tensor_tensor(
                out=sb["st_tL"][:, gg], in0=sb["st_sig"][:, gg], scalar=-8.0,
                in1=sb["st_mu"][:, gg], op0=ALU.mult, op1=ALU.add),
               reads=["st_sig", "st_mu"], writes=["st_tL"])
            OP("vec", lambda e, gg=g: e.memset(sb["st_tH"][:, gg], 1.0),
               [], ["st_tH"])
            OP("vec", lambda e, gg=g: e.memset(sb["st_cH"][:, gg], 0.0),
               [], ["st_cH"])
            for zc, tn in (("c_zlo", "st_t"), ("c_zhi", "st_t2")):
                OP("vec", lambda e, z=zc, gg=g, t_=tn: e.tensor_tensor(
                    out=sb[t_][:, gg], in0=sb["st_sig"][:, gg],
                    in1=sb[z][:, gg], op=ALU.mult),
                   reads=["st_sig", zc], writes=[tn])
                OP("vec", lambda e, gg=g, t_=tn: e.tensor_tensor(
                    out=sb[t_][:, gg], in0=sb[t_][:, gg],
                    in1=sb["st_mu"][:, gg], op=ALU.add),
                   reads=[tn, "st_mu"], writes=[tn])
            probe_counts(g0, tiles, pr=0)
            probe_counts(g0, tiles, pr=1)
            zupdate(g, dsl, asl, pr=0)
            zupdate(g, dsl, asl, pr=1)
            OP("vec", lambda e, gg=g: e.tensor_tensor(
                out=sb["st_w0"][:, gg], in0=sb["st_tH"][:, gg],
                in1=sb["st_tL"][:, gg], op=ALU.subtract),
               reads=["st_tH", "st_tL"], writes=["st_w0"])
            OP("vec", lambda e, gg=g: e.scalar_tensor_tensor(
                out=sb["st_t"][:, gg], in0=sb["st_w0"][:, gg], scalar=0.5,
                in1=sb["st_tL"][:, gg], op0=ALU.mult, op1=ALU.add),
               reads=["st_w0", "st_tL"], writes=["st_t"])
            for lv in range(NB):
                probe_counts(g0, tiles, pr=lv % 3)
                bupdate(g, dsl, asl, lv)
            for T in tiles:
                if T not in ACT_TILES:
                    continue
                jW = 128 * (T + 1)
                off = joff(T, g0)
                OP("vec", lambda e, o=off, w=jW, T_=T: e.tensor_scalar(
                    out=JUNK16[:, 0:w], in0=S16[:, o:o + w],
                    scalar1=sb["st_tH"][:, T_:T_ + 1], scalar2=None,
                    op0=ALU.is_ge, op1=ALU.add,
                    accum_out=sb["st_cH"][:, T_:T_ + 1]),
                   reads=["s16", "st_tH"], writes=["junk16", "st_cH"])
            OP("vec", lambda e, gg=g: e.tensor_scalar(
                out=sb["st_r"][:, gg], in0=sb["st_cH"][:, gg], scalar1=-1.0,
                scalar2=203.0, op0=ALU.mult, op1=ALU.add),
               reads=["st_cH"], writes=["st_r"])
            OP("vec", lambda e, gg=g: e.tensor_scalar(
                out=sb["st_r"][:, gg], in0=sb["st_r"][:, gg], scalar1=0.0,
                scalar2=15.0, op0=ALU.max, op1=ALU.min),
               reads=["st_r"], writes=["st_r"])
            for T in tiles:
                extract(T, g0)

        OP("vec", lambda e: e.tensor_tensor(
            out=sb["st_T"][:, 1:16], in0=sb["st_T"][:, 1:16],
            in1=sb["c_dm"][:, 1:16], op=ALU.mult),
           reads=["st_T", "c_dm"], writes=["st_T"])
        OP("vec", lambda e: e.tensor_tensor(
            out=sb["st_T"][:, 1:16], in0=sb["st_T"][:, 1:16],
            in1=sb["c_do"][:, 1:16], op=ALU.add),
           reads=["st_T", "c_do"], writes=["st_T"])
        if "dec" in taps and b == 0:
            DMA("pol", dbg["dec_T"][h], sb["st_T"][:], ["st_T"], [])
            DMA("pol", dbg["dec_M"][h], sb["st_negm"][:], ["st_negm"], [])

        for (mat, dscr, dres) in (("st_negm", scr["mrow_d"], "mrow_d"),
                                  ("st_T", scr["trow_d"], "trow_d")):
            OP("ten", lambda e, m=mat: e.transpose(
                ps2[0:16, 1536:1664], sb[m][:], sb["c_id"][:]),
               reads=[mat, "c_id"], writes=[PS2B[3]])
            OP("scl", lambda e: e.activation(
                out=sb["rows"][0:16, 0:128], in_=ps2[0:16, 1536:1664],
                func=AF.Identity, bias=0.0, scale=1.0),
               reads=[PS2B[3]], writes=["rows"])
            DMA("pol", dscr, sb["rows"][0:16, 0:128], ["rows"], [dres])
        DMA("sp", sb["fsc1"][0:1, :],
            bass.AP(scr["mrow_d"].tensor, 0, [[2048, 1], [1, 2048]]),
            ["mrow_d"], ["fsc1"])
        OP("vec", lambda e: e.tensor_copy(
            out=sb["fsc2"][0:1, 0:1024].bitcast(BF16),
            in_=sb["fsc1"][0:1, :]), reads=["fsc1"], writes=["fsc2"])
        DMA("sp", sb["fsc1"][0:1, :],
            bass.AP(scr["trow_d"].tensor, 0, [[2048, 1], [1, 2048]]),
            ["trow_d"], ["fsc1"])

    def epilogue(b, h):
        eplvl = 99
        for t_ in taps:
            if t_.startswith("ep") and t_[2:].isdigit():
                eplvl = int(t_[2:])
        f = h // 2
        r0 = (h % 2) * 64
        tp = (64, 0) if r0 else None
        qt, kt = QK[f], KK[f]
        qr, kr = f"q{f}", f"k{f}"
        trow = sb["fsc1"][0:1, :]
        mrowb = sb["fsc2"][0:1, 0:1024].bitcast(BF16)
        for qb in range(4):
            q0c = qb * 512
            OP("ten", lambda e, s=q0c: MM(
                e, ps2[:, 1024:1536], sb["onesf"][:], trow[0:1, s:s + 512],
                start=True, stop=True),
               reads=["onesf", "fsc1"], writes=[PS2B[2]])
            OP("scl", lambda e: e.activation(
                out=TBC, in_=ps2[:, 1024:1536], func=AF.Identity,
                bias=0.0, scale=1.0), reads=[PS2B[2]], writes=["tbc"])
            njt = 4 * qb + 4
            if eplvl < 2:
                continue
            pv_pending = []
            for jt in range(njt):
                p_ = 0 if 'nopar' in taps else jt % 2
                pb_ = 0 if 'nobank' in taps else p_
                SC = ps2[:, 0:512] if pb_ == 0 else ps2[:, 1536:2048]
                SCR = PS2B[0] if pb_ == 0 else PS2B[3]
                EB, IB, PB = (EBF, INDBF, PBF) if p_ == 0 else \
                    (EBF2, INDBF2, PBF2)
                ebr, ibr, pbr = ("ebf", "indbf", "pbf") if p_ == 0 else \
                    ("ebf2", "indbf2", "pbf2")
                OP("ten", lambda e, j=jt, s=q0c, sc=SC, sr=SCR: MM(
                    e, sc, kt[r0:r0 + 64, j * 128:j * 128 + 128],
                    qt[r0:r0 + 64, s:s + 512], start=True, stop=False,
                    tile_position=tp),
                   reads=[qr, kr], writes=[SCR])
                OP("ten", lambda e, s=q0c, sc=SC: MM(
                    e, sc, sb["onesb"][:], mrowb[0:1, s:s + 512],
                    start=False, stop=True),
                   reads=["onesb", "fsc2"], writes=[SCR])
                if pv_pending:
                    pv_pending.pop()()
                OP("scl", lambda e, sc=SC, eb=EB: e.activation(
                    out=eb, in_=sc, func=AF.Exp,
                    bias=0.0, scale=1.0), reads=[SCR], writes=[ebr])
                if eplvl < 3:
                    continue
                OP("vec", lambda e, sc=SC, ib=IB: e.tensor_tensor(
                    out=ib, in0=sc, in1=TBC, op=ALU.is_ge),
                   reads=[SCR, "tbc", ebr], writes=[ibr])
                if eplvl == 31:
                    continue
                OP("vec", lambda e, ib=IB, eb=EB, pb=PB: e.tensor_tensor(
                    out=pb, in0=ib, in1=eb, op=ALU.mult),
                   reads=[ibr, ebr], writes=[pbr])
                if eplvl == 32:
                    continue
                if jt >= 4 * qb:
                    oi = jt - 4 * qb
                    OP("vec", lambda e, o=oi, pb=PB: e.tensor_tensor(
                        out=pb, in0=pb, in1=CCA[o], op=ALU.mult),
                       reads=[pbr, f"c_ca{oi}"], writes=[pbr])
                if eplvl < 4:
                    continue
                def _pv(j=jt, n=njt, hh=h, pb=PB, pr_=pbr):
                    OP("ten", lambda e, j=j, n=n, hh=hh, pb=pb: MM(
                        e, ps2[0:65, 512:1024],
                        VT[:, j * VBLK + hh * VSTRIDE:
                           j * VBLK + hh * VSTRIDE + 65],
                        pb, start=(j == 0), stop=(j == n - 1)),
                       reads=["vt", "vt_ones", pr_], writes=[PS2B[1]])
                pv_pending.append(_pv)
            if eplvl >= 4 and pv_pending:
                pv_pending.pop()()
            if eplvl < 5:
                continue
            OP("scl", lambda e: e.activation(
                out=sb["fsc1"][64:65, 512:1024], in_=ps2[64:65, 512:1024],
                func=AF.Identity, bias=0.0, scale=1.0),
               reads=[PS2B[1]], writes=["zrow"])
            DMA("pol", scr["rowbc"][2:3, 0:512], sb["fsc1"][64:65, 512:1024],
                ["zrow"], ["rb2"])
            DMA("pol", RZBC, scr["rowbc"][2:3, 0:512].to_broadcast([64, 512]),
                ["rb2"], ["rzbc"])
            OP("vec", lambda e: e.reciprocal(
                out=RZBC, in_=RZBC), reads=["rzbc"], writes=["rzbc"])
            if eplvl < 6:
                continue
            OP("vec", lambda e: e.tensor_tensor(
                out=ABF1, in0=ps2[0:64, 512:1024], in1=RZBC, op=ALU.mult),
               reads=[PS2B[1], "rzbc"], writes=["abf1"])
            DMA("pol", scr["attn_d"][h * 64:h * 64 + 64, q0c:q0c + 512],
                ABF1, ["abf1"], ["attnd"])

    def proj_residual(b):
        for f in range(8):
            load_x_chunk(b, f)
            for cpair in range(4):
                DMA("sp", WFULL,
                    wstage_ap(scr["wp_bf"], cpair * 256, 2, DM), [], ["wfull"])
                for cc in range(2):
                    c = cpair * 2 + cc
                    for qb in range(4):
                        q0c = qb * 512
                        pp = (c * 4 + qb) % 2
                        AB = ATTBF if pp == 0 else ATTBF2
                        abr = "attbf" if pp == 0 else "attbf2"
                        DMA("sp", AB,
                            scr["attn_d"][c * 128:(c + 1) * 128,
                                          q0c:q0c + 512],
                            ["attnd"], [abr])
                        OP("ten", lambda e, ff=f, c2=cc, ci2=c, s=q0c,
                           ab=AB, ar=abr:
                           MM(e, ps2[:, s:s + 512],
                              WFULL[:, c2 * DM + ff * 128:
                                    c2 * DM + ff * 128 + 128],
                              ab, start=(ci2 == 0), stop=(ci2 == 7)),
                           reads=["wfull", abr],
                           writes=p2b(q0c, q0c + 512))
            for qb in range(4):
                q0c = qb * 512
                OP("vec", lambda e, ff=f, s=q0c: e.scalar_tensor_tensor(
                    out=XP1[ff][:, s:s + 512], in0=ps2[:, s:s + 512],
                    scalar=sb["projb_t"][:, ff:ff + 1],
                    in1=sb["xchunk"][:, s:s + 512], op0=ALU.add, op1=ALU.add),
                   reads=p2b(q0c, q0c + 512) + ["projb_t", "xchunk"],
                   writes=[f"xp1_{f}"])
        if "xp1" in taps and b == 0:
            for f in range(8):
                DMA("pol", dbg["xp1"][f * 128:(f + 1) * 128, :], XP1[f],
                    [f"xp1_{f}"], [])

    def ffn(b):
        ln_stats(lambda ci: (XP1[ci], f"xp1_{ci}"), MUBC2, RSTDBC2)
        DMA("sp", GATEW,
            wstage_ap(ext["gate_wt"], 0, 8, 16),
            [], ["gatew"])
        for ci in range(8):
            OP("vec", lambda e, c=ci: e.tensor_tensor(
                out=sb["fsc1"][:], in0=XP1[c], in1=MUBC2, op=ALU.subtract),
               reads=[f"xp1_{ci}", "mubc"], writes=["fsc1"])
            OP("vec", lambda e: e.tensor_tensor(
                out=sb["fsc2"][:], in0=sb["fsc1"][:], in1=RSTDBC2,
                op=ALU.mult), reads=["fsc1", "rstdbc"], writes=["fsc2"])
            OP("pol", lambda e, c=ci: e.tensor_copy(
                out=LN2[c], in_=sb["fsc2"][:]),
               reads=["fsc2"], writes=[f"ln2_{ci}"])
            for nbk in range(4):
                s0, s1 = nbk * 512, nbk * 512 + 512
                OP("ten", lambda e, c=ci, x0=s0, x1=s1: MM(
                    e, ps1[0:16, x0:x1], GATEW[:, c * 16:(c + 1) * 16],
                    sb["fsc2"][:, x0:x1], start=(c == 0), stop=(c == 7)),
                   reads=["gatew", "fsc2"], writes=p1b(s0, s1))
        OP("scl", lambda e: e.activation(
            out=sb["rows"][0:16, :], in_=ps1[0:16, :], func=AF.Sigmoid,
            bias=0.0, scale=1.0), reads=PS1B, writes=["rows"])
        for tt in range(NT):
            OP("ten", lambda e, t=tt: e.transpose(
                ps2[:, 1536:1552], sb["rows"][0:16, t * 128:(t + 1) * 128],
                sb["c_id"][0:16, 0:16]),
               reads=["rows", "c_id"], writes=[PS2B[3]])
            OP("scl", lambda e: e.activation(
                out=sb["st_s1"][:], in_=ps2[:, 1536:1552], func=AF.Identity,
                bias=0.0, scale=1.0), reads=[PS2B[3]], writes=["st_s1"])
            OP("vec", lambda e: e.max(out=sb["m8a"][:], in_=sb["st_s1"][:]),
               reads=["st_s1"], writes=["m8a"])
            OP("vec", lambda e: e.max_index(
                out=sb["m8i"][:], in_max=sb["m8a"][:],
                in_values=sb["st_s1"][:]),
               reads=["m8a", "st_s1"], writes=["m8i"])
            OP("vec", lambda e: e.tensor_copy(
                out=sb["m8b"][:], in_=sb["m8i"][:]),
               reads=["m8i"], writes=["m8b"])
            OP("vec", lambda e: e.tensor_tensor(
                out=sb["st_r"][:, 0:1], in0=sb["m8a"][:, 0:1],
                in1=sb["m8a"][:, 1:2], op=ALU.add),
               reads=["m8a"], writes=["st_r"])
            OP("vec", lambda e: e.tensor_scalar(
                out=sb["st_r"][:, 0:1], in0=sb["st_r"][:, 0:1],
                scalar1=1e-20, scalar2=None, op0=ALU.add),
               reads=["st_r"], writes=["st_r"])
            OP("vec", lambda e: e.reciprocal(
                out=sb["st_r"][:, 1:2], in_=sb["st_r"][:, 0:1]),
               reads=["st_r"], writes=["st_r"])
            for wi in range(2):
                OP("vec", lambda e, w=wi: e.tensor_tensor(
                    out=sb["st_r"][:, 2 + w:3 + w], in0=sb["m8a"][:, w:w + 1],
                    in1=sb["st_r"][:, 1:2], op=ALU.mult),
                   reads=["m8a", "st_r"], writes=["st_r"])
                OP("vec", lambda e, w=wi: e.tensor_scalar(
                    out=sb["st_r"][:, 2 + w:3 + w],
                    in0=sb["st_r"][:, 2 + w:3 + w], scalar1=2.5,
                    scalar2=None, op0=ALU.mult),
                   reads=["st_r"], writes=["st_r"])
            OP("vec", lambda e: e.tensor_scalar(
                out=sb["st_tmp"][:], in0=sb["c_iota"][:],
                scalar1=sb["m8b"][:, 0:1], scalar2=None, op0=ALU.is_equal),
               reads=["c_iota", "m8b"], writes=["st_tmp"])
            OP("vec", lambda e: e.tensor_scalar(
                out=sb["st_tmp"][:], in0=sb["st_tmp"][:],
                scalar1=sb["st_r"][:, 2:3], scalar2=None, op0=ALU.mult),
               reads=["st_tmp", "st_r"], writes=["st_tmp"])
            OP("vec", lambda e: e.tensor_scalar(
                out=sb["st_tmp2"][:], in0=sb["c_iota"][:],
                scalar1=sb["m8b"][:, 1:2], scalar2=None, op0=ALU.is_equal),
               reads=["c_iota", "m8b"], writes=["st_tmp2"])
            OP("vec", lambda e: e.tensor_scalar(
                out=sb["st_tmp2"][:], in0=sb["st_tmp2"][:],
                scalar1=sb["st_r"][:, 3:4], scalar2=None, op0=ALU.mult),
               reads=["st_tmp2", "st_r"], writes=["st_tmp2"])
            OP("vec", lambda e: e.tensor_tensor(
                out=sb["st_tmp"][:], in0=sb["st_tmp"][:],
                in1=sb["st_tmp2"][:], op=ALU.add),
               reads=["st_tmp", "st_tmp2"], writes=["st_tmp"])
            OP("ten", lambda e, t=tt: e.transpose(
                ps2[0:16, 1664:1792], sb["st_tmp"][:], sb["c_id"][:]),
               reads=["st_tmp", "c_id"], writes=[PS2B[3]])
            OP("scl", lambda e, t=tt: e.activation(
                out=sb["rows"][0:16, t * 128:(t + 1) * 128],
                in_=ps2[0:16, 1664:1792], func=AF.Identity, bias=0.0,
                scale=1.0), reads=[PS2B[3]], writes=["rows"])
        if "route" in taps and b == 0:
            DMA("pol", dbg["route"], sb["rows"][0:16, :], ["rows"], [])

        def expert(use_route):
            for mt in range(4):
                for c in range(8):
                    for nbk in range(4):
                        s0, s1 = nbk * 512, nbk * 512 + 512
                        OP("ten", lambda e, cc=c, m=mt, x0=s0, x1=s1: MM(
                            e, ps1[:, x0:x1],
                            WSTG[:, cc * 512 + m * 128:
                                 cc * 512 + m * 128 + 128],
                            LN2[cc][:, x0:x1], start=(cc == 0),
                            stop=(cc == 7)),
                           reads=["wstg", f"ln2_{c}"], writes=p1b(s0, s1))
                OP("scl", lambda e: e.activation(
                    out=sb["fsc2"][:], in_=ps1[:], func=AF.Silu, bias=0.0,
                    scale=1.0), reads=PS1B, writes=["fsc2"])
                for c in range(8):
                    for nbk in range(4):
                        s0, s1 = nbk * 512, nbk * 512 + 512
                        OP("ten", lambda e, cc=c, m=mt, x0=s0, x1=s1: MM(
                            e, ps2[:, x0:x1],
                            WSTU[:, cc * 512 + m * 128:
                                 cc * 512 + m * 128 + 128],
                            LN2[cc][:, x0:x1], start=(cc == 0),
                            stop=(cc == 7)),
                           reads=["wstu", f"ln2_{c}"], writes=p2b(s0, s1))
                OP("vec", lambda e: e.tensor_tensor(
                    out=sb["fsc1"][:], in0=ps2[:], in1=sb["fsc2"][:],
                    op=ALU.mult), reads=PS2B + ["fsc2"], writes=["fsc1"])
                if use_route:
                    OP("vec", lambda e, m=mt: e.tensor_tensor(
                        out=SU[m], in0=sb["fsc1"][:], in1=WEBC, op=ALU.mult),
                       reads=["fsc1", "webc"], writes=[f"su{mt}"])
                else:
                    OP("pol", lambda e, m=mt: e.tensor_copy(
                        out=SU[m], in_=sb["fsc1"][:]),
                       reads=["fsc1"], writes=[f"su{mt}"])
            for ht in range(8):
                psd = ps1 if ht % 2 == 0 else ps2
                pfn = p1b if ht % 2 == 0 else p2b
                PSB_ = PS1B if ht % 2 == 0 else PS2B
                for mc in range(4):
                    for nbk in range(4):
                        s0, s1 = nbk * 512, nbk * 512 + 512
                        OP("ten", lambda e, m=mc, hh=ht, x0=s0, x1=s1,
                           pd=psd: MM(
                            e, pd[:, x0:x1],
                            WSTD[:, m * DM + hh * 128:m * DM + hh * 128 + 128],
                            SU[m][:, x0:x1], start=(m == 0), stop=(m == 3)),
                           reads=["wstd", f"su{mc}"], writes=pfn(s0, s1))
                OP("vec", lambda e, hh=ht, pd=psd: e.tensor_tensor(
                    out=XP1[hh], in0=pd[:], in1=XP1[hh], op=ALU.add),
                   reads=PSB_ + [f"xp1_{ht}"], writes=[f"xp1_{ht}"])

        DMA("sp", WSTG, wstage_ap(scr["wsg_bf"], 0, 8, 512),
            [], ["wstg", "mubc"])
        DMA("sp", WSTU, wstage_ap(scr["wsu_bf"], 0, 8, 512),
            [], ["wstu", "rstdbc"])
        DMA("sp", WSTD, wstage_ap(scr["wsd_bf"], 0, 4, DM),
            [], ["wstd"])
        expert(False)
        for e_i in range(16):
            DMA("sp", WSTG,
                wstage_ap(scr["wg_bf"][e_i], 0, 8, 512),
                [], ["wstg", "mubc"])
            DMA("sp", WSTU,
                wstage_ap(scr["wu_bf"][e_i], 0, 8, 512),
                [], ["wstu", "rstdbc"])
            DMA("sp", WSTD,
                wstage_ap(scr["wd_bf"][e_i], 0, 4, DM),
                [], ["wstd"])
            DMA("pol", scr["rowbc"][3:4, :], sb["rows"][e_i:e_i + 1, :],
                ["rows"], ["rb3"])
            DMA("pol", WEBC, scr["rowbc"][3:4, :].to_broadcast([128, SEQ]),
                ["rb3"], ["webc"])
            expert(True)
        for ht in range(8):
            DMA("sp", out[b, ht * 128:(ht + 1) * 128, :], XP1[ht],
                [f"xp1_{ht}"], [])

    nheads = 1 if "oneh" in taps else HEADS
    nb_ = 0 if "p0only" in taps else (1 if "oneb" in taps else B_PER_CORE)
    for b in range(nb_):
        projections(b)
        ctx.barrier()
        if "projonly" in taps:
            break
        DMA("sp", C_DIAG, cst["diagmask"], [], ["c_diag"])
        for oi in range(4):
            DMA("sp", CCA[oi], cst[f"causal01_{oi}"], [], [f"c_ca{oi}"])
        for h in range(nheads):
            decision(b, h)
            if "noepi" not in taps:
                epilogue(b, h)
        ctx.barrier()
        if "attnonly" in taps:
            break
        proj_residual(b)
        ctx.barrier()
        if "noffn" in taps:
            continue
        ffn(b)
        ctx.barrier()

    ctx.final_wait("vec")
    ctx.final_wait("sp")
    ctx.enforce_dma_order()

    with ExitStack() as es2:
        sems = {}
        for qn in ("sp", "vec", "scl", "ten", "pol"):
            for k in ("op", "dma"):
                sems[(qn, k)] = es2.enter_context(nc.semaphore(f"{qn}_{k}"))
        blk = es2.enter_context(nc.Block())
        blk.sync(lambda e: ctx.replay("sp", e, sems))
        blk.vector(lambda e: ctx.replay("vec", e, sems))
        blk.scalar(lambda e: ctx.replay("scl", e, sems))
        blk.tensor(lambda e: ctx.replay("ten", e, sems))
        blk.gpsimd(lambda e: ctx.replay("pol", e, sems))

    stack.close()
    return nc


_cache = {}


def prep_inputs(L0_attn_x, L0_attn_qkv_w, L0_attn_proj_w, L0_attn_proj_b,
                L0_attn_kernel_q, L0_attn_kernel_k, L0_attn_kernel_v,
                L0_attn_gamma_freq, L0_attn_theta_freq, L0_attn_phase,
                L0_ffn_gate_w, L0_ffn_w_gate, L0_ffn_w_up, L0_ffn_w_down,
                L0_ffn_ws_gate, L0_ffn_ws_up, L0_ffn_ws_down):
    def t32(a):
        return np.ascontiguousarray(np.asarray(a, dtype=np.float32))

    x = t32(L0_attn_x)
    qkv_w = np.asarray(L0_attn_qkv_w, dtype=np.float32)
    B = x.shape[0]
    ncore = 8
    bpc = B // ncore
    xs = [np.ascontiguousarray(x[c * bpc:(c + 1) * bpc].transpose(0, 2, 1))
          for c in range(ncore)]
    base = {
        "wq_t": t32(qkv_w[0:DM].T),
        "wk_t": t32(qkv_w[DM:2 * DM].T),
        "wv_t": t32(qkv_w[2 * DM:3 * DM].T),
        "wp_t": t32(np.asarray(L0_attn_proj_w, np.float32).T),
        "proj_b": t32(np.asarray(L0_attn_proj_b, np.float32)
                      .reshape(8, 128).T),
        "kern3": np.broadcast_to(np.concatenate(
            [t32(L0_attn_kernel_q), t32(L0_attn_kernel_k),
             t32(L0_attn_kernel_v)]).reshape(1, 12), (128, 12)).copy(),
        "gamma_c": t32(np.asarray(L0_attn_gamma_freq, np.float32)
                       .reshape(8, 128).T),
        "phase_c": t32(np.asarray(L0_attn_phase, np.float32)
                       .reshape(8, 128).T),
        "theta_c": np.broadcast_to(
            t32(L0_attn_theta_freq).reshape(1, 1), (128, 1)).copy(),
        "gate_wt": t32(np.asarray(L0_ffn_gate_w, np.float32).T),
        "w_gate": t32(L0_ffn_w_gate), "w_up": t32(L0_ffn_w_up),
        "w_down": t32(L0_ffn_w_down), "ws_gate": t32(L0_ffn_ws_gate),
        "ws_up": t32(L0_ffn_ws_up), "ws_down": t32(L0_ffn_ws_down),
    }
    return base, xs


def kernel(**inputs):
    base, xs = prep_inputs(**inputs)
    if "prog" not in _cache:
        _cache["prog"] = build_program(build_consts(), taps=TAPS)
    nc = _cache["prog"]
    in_maps = [dict(base, x=xs[c]) for c in range(8)]
    res = run_bass_kernel_spmd(nc, in_maps, core_ids=list(range(8)))
    outs = []
    for c in range(8):
        o = np.asarray(res.results[c]["out"])
        outs.append(np.ascontiguousarray(o.transpose(0, 2, 1)))
    return np.concatenate(outs, axis=0).astype(np.float32)
